# version 01 (final)
# baseline (speedup 1.0000x reference)
import numpy as np
import ml_dtypes
from contextlib import ExitStack

import concourse.bass as bass
import concourse.mybir as mybir
import concourse.dve_ops as dve_ops
from concourse.dve_op import DveOp
from concourse.dve_spec import Spec, Src0, Src1, C0, C1, C2, Zero, maxx, minn
from concourse.dve_spec import lower as dve_lower
from concourse.dve_table_gen import dve_ver_for
from concourse.bass_utils import run_bass_kernel_spmd
from concourse.library_overlay import lower_extended_insts

F32 = mybir.dt.float32
F16 = mybir.dt.float16
BF16 = mybir.dt.bfloat16
AF = mybir.ActivationFunctionType
ALU = mybir.AluOpType
AX = mybir.AxisListType

D = 2048
S = 2048
I = 8192
NH = 16
NKV = 4
DH = 128
NCORES = 8

MAGIC = 12582912.0
QMAX = 127.0
SOFT_CAP = 50.0
EXP_SHIFT = -30.0
EM = float(np.exp(np.float64(-SOFT_CAP + EXP_SHIFT)))
SQRT_DH = float(np.sqrt(np.float64(DH)))
LN_EPS = 1e-5
Q_EPS = 1e-8


def _register_round_ops():
    have = {op.name: op for op in dve_ops.OPS}
    out = []
    ver = dve_ver_for("TRN2")
    specs = (
        ("RND_SCALE_ANT",
         minn(maxx((Src0 * C0 + C2) - C2, Zero - C1), C1),
         lambda in0, in1, s0, s1, imm2: np.clip(np.round(in0 * s0), -s1, s1)),
        ("RND_TSCALE_ANT",
         minn(maxx((Src0 * Src1 + C2) - C2, Zero - C0), C0),
         lambda in0, in1, s0, s1, imm2: np.clip(np.round(in0 * in1), -s0, s0)),
    )
    for name, body, ref in specs:
        if name in have:
            out.append(have[name])
            continue
        spec = Spec(body=body)
        compiled = dve_lower(spec, ver=ver, perf_en=False, name=name)
        op = DveOp(name, spec, subdim=False, uops_sha={ver: compiled.sha(ver)},
                   reference=ref)
        dve_ops.OPS.append(op)
        dve_ops.CUSTOM_DVE_SPECS[name] = spec
        dve_ops.CUSTOM_DVE_REFS[name] = op.ref
        dve_ops._SUB_OPCODE_FOR_NAME[name] = (
            dve_ops._CUSTOM_DVE_ROW_BASE + len(dve_ops.OPS) - 1)
        out.append(op)
    return out


RND1, RND2 = _register_round_ops()


class Sync:

    def __init__(self):
        self.c = {}
        self.sems = {}

    def add(self, name, sem):
        self.sems[name] = sem
        self.c[name] = 0

    def inc(self, instr, name, n=1):
        instr.then_inc(self.sems[name], n)
        self.c[name] += n
        return (name, self.c[name])

    def wait(self, eng, ev):
        if ev is None:
            return
        name, val = ev
        assert val <= self.c[name], f"wait on future {name}:{val}>{self.c[name]}"
        if val > 0:
            eng.wait_ge(self.sems[name], val)

    def now(self, name):
        return (name, self.c[name])


def build_nc(phases="full"):
    nc = bass.Bass("TRN2", target_bir_lowering=False,
                   detect_race_conditions=False)

    xT = nc.dram_tensor("xT", [D, S], F32, kind="ExternalInput").ap()
    wq = nc.dram_tensor("wq", [D, D], F32, kind="ExternalInput").ap()
    wk = nc.dram_tensor("wk", [D, NKV * DH], F32, kind="ExternalInput").ap()
    wv = nc.dram_tensor("wv", [D, NKV * DH], F32, kind="ExternalInput").ap()
    wo = nc.dram_tensor("wo", [D, D], F32, kind="ExternalInput").ap()
    w1n = nc.dram_tensor("w1n", [I, D], F32, kind="ExternalInput").ap()
    w1tb = nc.dram_tensor("w1tb", [64, D, 128], F32, kind="ExternalInput").ap()
    w2n = nc.dram_tensor("w2n", [D, I], F32, kind="ExternalInput").ap()
    w2tb = nc.dram_tensor("w2tb", [16, I, 128], F32, kind="ExternalInput").ap()
    b1r = nc.dram_tensor("b1r", [128, 64], F32, kind="ExternalInput").ap()
    b2r = nc.dram_tensor("b2r", [128, 16], F32, kind="ExternalInput").ap()
    maskT = nc.dram_tensor("maskT", [128, 128], F32, kind="ExternalInput").ap()
    cntrow = nc.dram_tensor("cntrow", [1, S], F32, kind="ExternalInput").ap()
    identb = nc.dram_tensor("identb", [128, 128], BF16, kind="ExternalInput").ap()
    identf = nc.dram_tensor("identf", [128, 128], F32, kind="ExternalInput").ap()

    yout = nc.dram_tensor("yout", [D, S], F32, kind="ExternalOutput").ap()
    dbg = phases.endswith("_dbg")
    if dbg:
        dbg_scal = nc.dram_tensor("dbg_scal", [1, 8], F32,
                                  kind="ExternalOutput").ap()
        dbg_sw1 = nc.dram_tensor("dbg_sw1", [128, 64], F32,
                                 kind="ExternalOutput").ap()
        dbg_qy = nc.dram_tensor("dbg_qy", [128, 32768], F16,
                                kind="ExternalOutput").ap()
        dbg_y = nc.dram_tensor("dbg_y", [128, 32768], F16,
                               kind="ExternalOutput").ap()
        dbg_h = nc.dram_tensor("dbg_h", [I, 512], F16,
                               kind="ExternalOutput").ap()
        dbg_r = nc.dram_tensor("dbg_r", [128, 4], F32,
                               kind="ExternalOutput").ap()

    h2v = None
    if phases.startswith("attnb_h2"):
        h2v = phases.split(":", 1)[1] if ":" in phases else ""
    cdbg = phases == "upto_attn_cdbg"
    if h2v is not None:
        d_cls2 = nc.dram_tensor("d_cls2", [128, 2048], BF16,
                                kind="ExternalOutput").ap()
        d_zr2 = nc.dram_tensor("d_zr2", [1, 2048], F32,
                               kind="ExternalOutput").ap()
        d_psa2 = nc.dram_tensor("d_psa2", [128, 512], F32,
                                kind="ExternalOutput").ap()
        d_earb = nc.dram_tensor("d_earb", [128, 512], BF16,
                                kind="ExternalOutput").ap()
    if cdbg:
        d_call = nc.dram_tensor("d_call", [16, 128, 2048], BF16,
                                kind="ExternalOutput").ap()
        d_rall = nc.dram_tensor("d_rall", [16, 1, 2048], F32,
                                kind="ExternalOutput").ap()
        d_zp0 = nc.dram_tensor("d_zp0", [128, 2048], F32,
                               kind="ExternalOutput").ap()
        d_cls = nc.dram_tensor("d_cls", [128, 2048], BF16,
                               kind="ExternalOutput").ap()
        d_psa = nc.dram_tensor("d_psa", [128, 512], F32,
                               kind="ExternalOutput").ap()
        d_rb = nc.dram_tensor("d_rb", [128, 512], F32,
                              kind="ExternalOutput").ap()
        d_suf2 = nc.dram_tensor("d_suf2", [128, 64], F32,
                                kind="ExternalOutput").ap()
    h1d = phases == "attnb_h1d"
    if h1d:
        d_qb = nc.dram_tensor("d_qb", [128, 2048], BF16,
                              kind="ExternalOutput").ap()
        d_kar = nc.dram_tensor("d_kar", [128, 2048], BF16,
                               kind="ExternalOutput").ap()
        d_vt = nc.dram_tensor("d_vt", [128, 2048], BF16,
                              kind="ExternalOutput").ap()
        d_zp = nc.dram_tensor("d_zp", [128, 2048], F32,
                              kind="ExternalOutput").ap()
        d_zr = nc.dram_tensor("d_zr", [1, 2048], F32,
                              kind="ExternalOutput").ap()
        d_ctx = nc.dram_tensor("d_ctx", [128, 2048], BF16,
                               kind="ExternalOutput").ap()
        d_suf = nc.dram_tensor("d_suf", [128, 16], F32,
                               kind="ExternalOutput").ap()

    xscr = nc.dram_tensor("xscr", [D, S], F32).ap()
    hscr = nc.dram_tensor("hscr", [4, I, 512], F16).ap()
    qw1s = nc.dram_tensor("qw1s", [64, 128, 2048], F16).ap()
    qw2s = nc.dram_tensor("qw2s", [16, 128, 8192], F16).ap()
    ctxs = nc.dram_tensor("ctxs", [16, 128, 2048], BF16).ap()
    ccin = nc.dram_tensor("ccin", [1, 16], F32).ap()
    ccout = nc.dram_tensor("ccout", [1, 16], F32).ap()

    Y = Sync()

    with ExitStack() as top:
        E = top.enter_context
        for sn in ("qs", "qc", "qp", "st", "sv", "ss", "scc", "bar", "kd",
                   "k0", "k1", "k2", "k3", "k4", "k5", "k6", "k7", "k8",
                   "k9", "k10", "k11",
                   "g0", "g1", "g2", "g3", "g4", "g5", "g6", "g7"):
            Y.add(sn, E(nc.semaphore(sn)))

        psA = E(nc.psum_tensor("psA", [128, 2048], F32))
        psB = E(nc.psum_tensor("psB", [128, 1024], F32))
        psC = E(nc.psum_tensor("psC", [128, 512], F32))
        ps7 = E(nc.psum_tensor("ps7", [128, 512], F32))
        psZ = ps7

        ones_cf = E(nc.sbuf_tensor("ones_cf", [128, 1], F32))
        ones_rf = E(nc.sbuf_tensor("ones_rf", [1, 128], F32))
        ones_cb = E(nc.sbuf_tensor("ones_cb", [128, 1], BF16))
        identb_sb = E(nc.sbuf_tensor("identb_sb", [128, 128], BF16))
        identf_sb = E(nc.sbuf_tensor("identf_sb", [128, 128], F32))
        maskT_sb = E(nc.sbuf_tensor("maskT_sb", [128, 128], F32))
        cnt_sb = E(nc.sbuf_tensor("cnt_sb", [1, S], F32))
        shiftb = E(nc.sbuf_tensor("shiftb", [128, 1], F32))
        b1_sb = E(nc.sbuf_tensor("b1_sb", [128, 64], F32))
        b2_sb = E(nc.sbuf_tensor("b2_sb", [128, 16], F32))
        sw1c = E(nc.sbuf_tensor("sw1c", [128, 64], F32))
        sw2c = E(nc.sbuf_tensor("sw2c", [128, 16], F32))
        sc2 = E(nc.sbuf_tensor("sc2", [128, 16], F32))
        bc2 = E(nc.sbuf_tensor("bc2", [128, 16], F32))
        sgbuf = E(nc.sbuf_tensor("sgbuf", [128, 2], F32))
        r1b = E(nc.sbuf_tensor("r1b", [128, 1], F32))
        r2b = E(nc.sbuf_tensor("r2b", [128, 1], F32))
        sx1b = E(nc.sbuf_tensor("sx1b", [128, 1], F32))
        tcol = E(nc.sbuf_tensor("tcol", [128, 1], F32))
        scal = E(nc.sbuf_tensor("scal", [1, 8], F32))
        arst = E(nc.sbuf_tensor("arst", [1, 16], F32))
        gmaxr = E(nc.sbuf_tensor("gmaxr", [1, 16], F32))
        ymaxc = E(nc.sbuf_tensor("ymaxc", [128, 64], F32))
        hmaxc = E(nc.sbuf_tensor("hmaxc", [128, 64], F32))
        hmx4 = E(nc.sbuf_tensor("hmx4", [128, 4], F32))
        colmax = E(nc.sbuf_tensor("colmax", [128, 1], F32))
        swr2 = E(nc.sbuf_tensor("swr2", [1, 2048], F32))
        vjunk = E(nc.sbuf_tensor("vjunk", [128, 2], F32))
        mrow = E(nc.sbuf_tensor("mrow", [1, 512], F32))
        qrow = E(nc.sbuf_tensor("qrow", [1, 512], F32))
        trow = E(nc.sbuf_tensor("trow", [1, 512], F32))
        rrow = E(nc.sbuf_tensor("rrow", [1, 512], F32))
        brow = E(nc.sbuf_tensor("brow", [1, 512], F32))

        block = E(nc.Block())

        def vspace(v):
            v.tensor_copy(vjunk[:, 0:1], vjunk[:, 1:2])

        bar_n = [0]

        def barrier():
            tgt = dict(Y.c)
            bar_n[0] += 1
            join = 5 * bar_n[0]

            def flush(eng):
                for sname, v in tgt.items():
                    if v and sname != "bar":
                        eng.wait_ge(Y.sems[sname], v)
                Y.c["bar"] += 1
                eng.nop().then_inc(Y.sems["bar"], 1)
                eng.wait_ge(Y.sems["bar"], join)
            block.sync(flush)
            block.gpsimd(flush)
            block.scalar(flush)
            block.tensor(flush)
            block.vector(flush)
            assert Y.c["bar"] == join

        def _init_v(v):
            v.memset(vjunk[:, :], 0.0)
            v.memset(ones_cf[:, :], 1.0)
            v.memset(ones_rf[:, :], 1.0)
            v.memset(ones_cb[:, :], 1.0)
            v.memset(shiftb[:, :], EXP_SHIFT)
            Y.inc(v.memset(arst[:, :], 0.0), "sv")
        block.vector(_init_v)

        def _init_s(sy):
            for dst, src in ((identb_sb, identb), (identf_sb, identf),
                             (maskT_sb, maskT), (cnt_sb, cntrow),
                             (b1_sb, b1r), (b2_sb, b2r)):
                Y.inc(sy.dma_start(out=dst[:, :], in_=src[:, :]), "qs", 16)
        block.sync(_init_s)
        barrier()

        ar_n = [0]

        def allreduce_scale():
            ar_n[0] += 1
            k = ar_n[0]

            def v1(v):
                Y.inc(v.tensor_copy(arst[0:1, 0:1], scal[0:1, 0:1]), "sv")
            block.vector(v1)
            e_cp = Y.now("sv")

            def g1(g):
                Y.wait(g, e_cp)
                Y.inc(g.dma_start(out=ccin[:, :], in_=arst[:, :]), "qp", 16)
                Y.wait(g, Y.now("qp"))
                Y.inc(g.collective_compute(
                    "AllReduce", ALU.max,
                    replica_groups=[list(range(NCORES))],
                    ins=[ccin[:, :]], outs=[ccout[:, :]]), "scc")
                g.wait_ge(Y.sems["scc"], k)
                Y.inc(g.dma_start(out=gmaxr[:, :], in_=ccout[:, :]),
                      "g7", 16)
            block.gpsimd(g1)
            e_ld = Y.now("g7")

            def v2(v):
                Y.wait(v, e_ld)
                v.tensor_scalar(scal[0:1, 1:2], gmaxr[0:1, 0:1], 1.0 / QMAX,
                                Q_EPS, op0=ALU.mult, op1=ALU.max)
                vspace(v)
                Y.inc(v.reciprocal(scal[0:1, 2:3], scal[0:1, 1:2]), "sv")
            block.vector(v2)

        def bcast_cols(pairs):
            pre = Y.now("sv")

            def t1(t):
                Y.wait(t, pre)
                for j, (srcap, dst) in enumerate(pairs):
                    Y.inc(t.matmul(psC[:, j:j + 1], ones_rf[:, :], srcap,
                                   start=True, stop=True), "st")
            block.tensor(t1)
            e_mm = Y.now("st")

            def v1(v):
                Y.wait(v, e_mm)
                for j, (srcap, dst) in enumerate(pairs):
                    Y.inc(v.tensor_copy(dst[:, :], psC[:, j:j + 1]), "sv")
            block.vector(v1)

        def finalize_max(collect, width):
            def v1(v):
                Y.inc(v.tensor_reduce(colmax[:, :], collect[:, 0:width],
                                      axis=AX.X, op=ALU.max), "sv")
            block.vector(v1)
            e1 = Y.now("sv")

            def t1(t):
                Y.wait(t, e1)
                Y.inc(t.transpose(psZ[0:1, 0:128], colmax[:, :],
                                  identf_sb[:, :]), "st")
            block.tensor(t1)
            e2 = Y.now("st")

            def v2(v):
                Y.wait(v, e2)
                Y.inc(v.tensor_reduce(scal[0:1, 0:1], psZ[0:1, 0:128],
                                      axis=AX.X, op=ALU.max), "sv")
                vspace(v)
            block.vector(v2)

        ln_n = [0]

        def ln_phase(src, y_arena, do_max):
            ln_n[0] += 1
            u = ln_n[0]
            with ExitStack() as ph:
                EE = ph.enter_context
                xf = [EE(nc.sbuf_tensor(f"lnxf{i}_{u}", [128, 8192], F32))
                      for i in range(2)]
                x2 = EE(nc.sbuf_tensor(f"lnx2_{u}", [128, 4096], F32))
                tmpt = EE(nc.sbuf_tensor(f"lntmp_{u}", [128, 512], F32))
                mrow = EE(nc.sbuf_tensor(f"mrow_{u}", [1, 512], F32))
                qrow = EE(nc.sbuf_tensor(f"qrow_{u}", [1, 512], F32))
                trow = EE(nc.sbuf_tensor(f"trow_{u}", [1, 512], F32))
                rrow = EE(nc.sbuf_tensor(f"rrow_{u}", [1, 512], F32))
                brow = EE(nc.sbuf_tensor(f"brow_{u}", [1, 512], F32))

                e_apply = [None, None]
                e_stats_rd = None
                e_ab_rd = None
                e_sq = {}
                e_sqmm = {}
                e_ldf = [None, None]

                for t in range(4):
                    par = t % 2

                    def syl(sy, t=t, par=par):
                        Y.wait(sy, e_apply[par])
                        src3 = src[:, t * 512:(t + 1) * 512] \
                            .rearrange("(c p) n -> p c n", p=128)
                        dst3 = xf[par].rearrange("p (c n) -> p c n", n=512)
                        e_ldf[par] = Y.inc(sy.dma_start(out=dst3, in_=src3),
                                           f"k{par}", 16)
                    block.sync(syl)

                    def tsum(t_, t=t, par=par):
                        Y.wait(t_, e_ldf[par])
                        Y.wait(t_, e_stats_rd)
                        for c in range(16):
                            mm = t_.matmul(psA[0:1, 0:512], ones_cf[:, :],
                                           xf[par][:, c * 512:(c + 1) * 512],
                                           start=(c == 0), stop=(c == 15))
                        Y.inc(mm, "st")
                    block.tensor(tsum)
                    e_sum = Y.now("st")

                    for g in range(4):
                        gp_ = g % 2

                        def ssq(s, t=t, par=par, g=g, gp_=gp_):
                            Y.wait(s, e_ldf[par])
                            Y.wait(s, e_sqmm.get((t, g - 2) if g >= 2
                                                 else (t - 1, g + 2)))
                            last = None
                            for cc in range(4):
                                c = g * 4 + cc
                                last = s.activation(
                                    x2[:, gp_ * 2048 + cc * 512:
                                       gp_ * 2048 + (cc + 1) * 512],
                                    xf[par][:, c * 512:(c + 1) * 512],
                                    AF.Square)
                            e_sq[(t, g)] = Y.inc(last, "ss")
                        block.scalar(ssq)

                        def tsq(t_, t=t, g=g, gp_=gp_):
                            Y.wait(t_, e_sq[(t, g)])
                            if g == 0:
                                Y.wait(t_, e_stats_rd)
                            for cc in range(4):
                                c = g * 4 + cc
                                mm = t_.matmul(
                                    psA[0:1, 512:1024], ones_cf[:, :],
                                    x2[:, gp_ * 2048 + cc * 512:
                                       gp_ * 2048 + (cc + 1) * 512],
                                    start=(c == 0), stop=(c == 15))
                            e_sqmm[(t, g)] = Y.inc(mm, "st")
                        block.tensor(tsq)

                    def vst(v, t=t):
                        Y.wait(v, e_sum)
                        Y.wait(v, e_sqmm[(t, 3)])
                        v.tensor_scalar(mrow[:, :], psA[0:1, 0:512],
                                        1.0 / D, None, op0=ALU.mult)
                        v.tensor_scalar(qrow[:, :], psA[0:1, 512:1024],
                                        1.0 / D, None, op0=ALU.mult)
                        v.tensor_tensor(trow[:, :], mrow[:, :], mrow[:, :],
                                        op=ALU.mult)
                        v.tensor_tensor(qrow[:, :], qrow[:, :], trow[:, :],
                                        op=ALU.subtract)
                        v.tensor_scalar(qrow[:, :], qrow[:, :], LN_EPS, None,
                                        op0=ALU.add)
                        vspace(v)
                        Y.inc(v.reciprocal(qrow[:, :], qrow[:, :]), "sv")
                    block.vector(vst)
                    e_stats_rd = Y.now("sv")
                    e_rcp = Y.now("sv")

                    def ssr(s, t=t):
                        Y.wait(s, e_rcp)
                        Y.inc(s.activation(rrow[:, :], qrow[:, :], AF.Sqrt),
                              "ss")
                    block.scalar(ssr)
                    e_sqrt = Y.now("ss")

                    def vbr(v, t=t):
                        Y.wait(v, e_sqrt)
                        Y.inc(v.scalar_tensor_tensor(
                            brow[:, :], mrow[:, :], -1.0, rrow[:, :],
                            op0=ALU.mult, op1=ALU.mult), "sv")
                    block.vector(vbr)
                    e_brow = Y.now("sv")

                    def tab(t_, t=t):
                        Y.wait(t_, e_brow)
                        Y.wait(t_, e_ab_rd)
                        t_.matmul(psA[:, 1024:1536], ones_rf[:, :], rrow[:, :],
                                  start=True, stop=True)
                        Y.inc(t_.matmul(psA[:, 1536:2048], ones_rf[:, :],
                                        brow[:, :], start=True, stop=True),
                              "st")
                    block.tensor(tab)
                    e_ab = Y.now("st")

                    def vap(v, t=t, par=par):
                        Y.wait(v, e_ab)
                        Y.wait(v, e_ldf[par])
                        last = None
                        for c in range(16):
                            v.tensor_tensor(tmpt[:, :],
                                            xf[par][:, c * 512:(c + 1) * 512],
                                            psA[:, 1024:1536], op=ALU.mult)
                            last = v.tensor_tensor(
                                y_arena[:, c * 2048 + t * 512:
                                        c * 2048 + (t + 1) * 512],
                                tmpt[:, :], psA[:, 1536:2048], op=ALU.add)
                            if do_max:
                                last = v.tensor_reduce(
                                    ymaxc[:, t * 16 + c:t * 16 + c + 1],
                                    y_arena[:, c * 2048 + t * 512:
                                            c * 2048 + (t + 1) * 512],
                                    axis=AX.X, op=ALU.max,
                                    apply_absolute_value=True)
                        e_apply[par] = Y.inc(last, "sv")
                    block.vector(vap)
                    e_ab_rd = e_apply[par]
            barrier()

        def ffn_phase(sub, y_arena, src, dst, y_stack=None):
            prep = (sub == 1)

            def vq(v):
                last = None
                for j in range(64):
                    last = v._custom_dve(
                        RND1, out=y_arena[:, j * 512:(j + 1) * 512],
                        in0=y_arena[:, j * 512:(j + 1) * 512],
                        s0=r1b[:, :], s1=QMAX, imm2=MAGIC)
                Y.inc(last, "sv")
            if dbg and sub == 1:
                def dyy(sy):
                    for sn in ("sv",):
                        sy.wait_ge(Y.sems[sn], Y.c[sn])
                    Y.inc(sy.dma_start(out=dbg_y[:, :], in_=y_arena[:, :]),
                          "qs", 16)
                block.sync(dyy)
                def vqwait(v):
                    Y.wait(v, Y.now("qs"))
                block.vector(vqwait)
            block.vector(vq)
            e_qy = Y.now("sv")
            if dbg and sub == 1:
                def dqy(sy):
                    Y.wait(sy, e_qy)
                    Y.inc(sy.dma_start(out=dbg_qy[:, :], in_=y_arena[:, :]),
                          "qs", 16)
                block.sync(dqy)

            with ExitStack() as ph:
                EE = ph.enter_context
                if prep:
                    w1nat = [EE(nc.sbuf_tensor(f"w1nat{i}_{sub}", [128, 2048], F32))
                             for i in range(2)]
                    w1tbuf = [EE(nc.sbuf_tensor(f"w1tbuf{i}_{sub}", [128, 2048], F32))
                              for i in range(2)]
                    rswb = EE(nc.sbuf_tensor(f"rswb_{sub}", [128, 128], F32))
                    swr1 = EE(nc.sbuf_tensor(f"swr1_{sub}", [1, 8192], F32))
                stage = [EE(nc.sbuf_tensor(f"qw1st{i}_{sub}", [128, 2048], F16))
                         for i in range(2)]
                hstage = [EE(nc.sbuf_tensor(f"hst{i}_{sub}", [128, 2048], F16))
                          for i in range(2)]

                e_natld = [None, None]
                e_tbld = [None, None]
                e_stage = [None, None]
                e_qw1w = [None, None]
                e_stagerd = [None, None]
                e_hwr = [None, None]
                e_hmx = [None, None]
                e_gelu = [[None] * 4, [None] * 4]
                e_mm = [[None] * 4, [None] * 4]
                e_sg = [None, None]

                for ib in range(64):
                    par = ib % 2

                    if prep:
                        def syl(sy, ib=ib, par=par):
                            Y.wait(sy, e_stagerd[par])
                            Y.inc(sy.dma_start(
                                out=w1nat[par][:, :],
                                in_=w1n[ib * 128:(ib + 1) * 128, :]),
                                f"k{4 + par}", 16)
                            src3 = w1tb[ib].rearrange("(c p) f -> p c f", p=128)
                            dst3 = w1tbuf[par].rearrange("p (c f) -> p c f",
                                                         f=128)
                            e_natld[par] = e_tbld[par] = Y.inc(
                                sy.dma_start(out=dst3, in_=src3),
                                f"k{4 + par}", 16)
                        block.sync(syl)

                        def vp1(v, ib=ib, par=par):
                            Y.wait(v, e_natld[par])
                            v.tensor_reduce(colmax[:, :], w1nat[par][:, :],
                                            axis=AX.X, op=ALU.max,
                                            apply_absolute_value=True)
                            vspace(v)
                            Y.inc(v.tensor_scalar(
                                sw1c[:, ib:ib + 1], colmax[:, :], 1.0 / QMAX,
                                Q_EPS, op0=ALU.mult, op1=ALU.max), "sv")
                        block.vector(vp1)
                        e_sw = Y.now("sv")

                        def tp1(t, ib=ib):
                            Y.wait(t, e_sw)
                            Y.inc(t.transpose(psZ[0:1, 0:128],
                                              sw1c[:, ib:ib + 1],
                                              identf_sb[:, :]), "st")
                        block.tensor(tp1)
                        e_tr = Y.now("st")

                        def vp2(v, ib=ib):
                            Y.wait(v, e_tr)
                            v.tensor_copy(swr1[0:1, ib * 128:(ib + 1) * 128],
                                          psZ[0:1, 0:128])
                            Y.inc(v.reciprocal(
                                swr1[0:1, ib * 128:(ib + 1) * 128],
                                swr1[0:1, ib * 128:(ib + 1) * 128]), "sv")
                        block.vector(vp2)
                        e_rsw = Y.now("sv")

                        def tp2(t, ib=ib):
                            Y.wait(t, e_rsw)
                            Y.inc(t.matmul(psC[:, 128:256], ones_rf[:, :],
                                           swr1[0:1, ib * 128:(ib + 1) * 128],
                                           start=True, stop=True), "st")
                        block.tensor(tp2)
                        e_rswb = Y.now("st")

                        def vp3(v, ib=ib, par=par):
                            Y.wait(v, e_rswb)
                            v.tensor_copy(rswb[:, :], psC[:, 128:256])
                            Y.wait(v, e_tbld[par])
                            Y.wait(v, e_stagerd[par])
                            Y.wait(v, e_qw1w[par])
                            for c in range(16):
                                v._custom_dve(
                                    RND2,
                                    out=stage[par][:, c * 128:(c + 1) * 128],
                                    in0=w1tbuf[par][:, c * 128:(c + 1) * 128],
                                    in1=rswb[:, :], s0=QMAX, imm2=MAGIC)
                            Y.wait(v, e_gelu[par][3])
                            last = v.tensor_tensor(sgbuf[:, par:par + 1],
                                                   sw1c[:, ib:ib + 1],
                                                   sx1b[:, :], op=ALU.mult)
                            e_stage[par] = e_sg[par] = Y.inc(last, "sv")
                        block.vector(vp3)

                        def scw(s, ib=ib, par=par):
                            Y.wait(s, e_stage[par])
                            e_qw1w[par] = Y.inc(
                                s.dma_start(out=qw1s[ib],
                                            in_=stage[par][:, :]),
                                f"k{9 + par}", 16)
                        block.scalar(scw)
                    else:
                        def syl3(sy, ib=ib, par=par):
                            Y.wait(sy, e_stagerd[par])
                            e_stage[par] = Y.inc(sy.dma_start(
                                out=stage[par][:, :], in_=qw1s[ib]),
                                f"k{4 + par}", 16)
                        block.sync(syl3)

                        def vp3b(v, ib=ib, par=par):
                            Y.wait(v, e_gelu[par][3])
                            e_sg[par] = Y.inc(
                                v.tensor_tensor(sgbuf[:, par:par + 1],
                                                sw1c[:, ib:ib + 1],
                                                sx1b[:, :], op=ALU.mult), "sv")
                        block.vector(vp3b)

                    def tmm(t, ib=ib, par=par):
                        Y.wait(t, e_stage[par])
                        Y.wait(t, e_qy)
                        for tt_ in range(4):
                            Y.wait(t, e_gelu[1 - par][tt_])
                            for h in range(16):
                                mm = t.matmul(
                                    psA[:, tt_ * 512:(tt_ + 1) * 512],
                                    stage[par][:, h * 128:(h + 1) * 128],
                                    y_arena[:, h * 2048 + tt_ * 512:
                                            h * 2048 + (tt_ + 1) * 512],
                                    start=(h == 0), stop=(h == 15))
                            e_mm[par][tt_] = Y.inc(mm, "st")
                        e_stagerd[par] = Y.now("st")
                    block.tensor(tmm)

                    def sg_(s, ib=ib, par=par):
                        Y.wait(s, e_sg[par])
                        Y.wait(s, e_hwr[par])
                        Y.wait(s, e_hmx[par])
                        for tt_ in range(4):
                            Y.wait(s, e_mm[par][tt_])
                            e_gelu[par][tt_] = Y.inc(s.activation(
                                hstage[par][:, tt_ * 512:(tt_ + 1) * 512],
                                psA[:, tt_ * 512:(tt_ + 1) * 512], AF.Gelu,
                                bias=b1_sb[:, ib:ib + 1],
                                scale=sgbuf[:, par:par + 1]), "ss")
                    block.scalar(sg_)

                    def vh(v, ib=ib, par=par):
                        for tt_ in range(4):
                            Y.wait(v, e_gelu[par][tt_])
                            v.tensor_reduce(
                                hmx4[:, tt_:tt_ + 1],
                                hstage[par][:, tt_ * 512:(tt_ + 1) * 512],
                                axis=AX.X, op=ALU.max,
                                apply_absolute_value=True)
                        vspace(v)
                        e_hmx[par] = Y.inc(
                            v.tensor_reduce(hmaxc[:, ib:ib + 1], hmx4[:, :],
                                            axis=AX.X, op=ALU.max), "sv")
                    block.vector(vh)

                    def sch(s, ib=ib, par=par):
                        for tt_ in range(4):
                            Y.wait(s, e_gelu[par][tt_])
                            Y.inc(s.dma_start(
                                out=hscr[tt_][ib * 128:(ib + 1) * 128, :],
                                in_=hstage[par][:, tt_ * 512:(tt_ + 1) * 512]),
                                f"k{6 + par}", 16)
                        e_hwr[par] = Y.now(f"k{6 + par}")
                    block.scalar(sch)
            barrier()
            if y_stack is not None:
                y_stack.close()

            finalize_max(hmaxc, 64)
            allreduce_scale()
            bcast_cols([(scal[0:1, 2:3], r2b)])

            if prep:
                with ExitStack() as ph:
                    EE = ph.enter_context
                    w2natb = [EE(nc.sbuf_tensor(f"w2nat{i}_{sub}", [128, 4096], F32))
                              for i in range(2)]
                    w2tbuf = [EE(nc.sbuf_tensor(f"w2tb_{i}_{sub}", [128, 1024], F32))
                              for i in range(2)]
                    rswb2 = EE(nc.sbuf_tensor(f"rswb2_{sub}", [128, 128], F32))
                    swr2l = EE(nc.sbuf_tensor(f"swr2_{sub}", [1, 2048], F32))
                    wstg = [EE(nc.sbuf_tensor(f"wstg{i}_{sub}", [128, 1024], F16))
                            for i in range(2)]

                    e_nld = [None, None]
                    e_nrd = [None, None]
                    for ob in range(16):
                        parn = ob % 2

                        def syn(sy, ob=ob, parn=parn):
                            Y.wait(sy, e_nrd[parn])
                            ev1 = Y.inc(sy.dma_start(
                                out=w2natb[parn][:, :],
                                in_=w2n[ob * 128:(ob + 1) * 128, 0:4096]),
                                f"k{parn}", 16)
                            e_nld[parn] = (ev1, None)
                        block.sync(syn)

                        def vn1(v, ob=ob, parn=parn):
                            Y.wait(v, e_nld[parn][0])
                            Y.inc(v.tensor_reduce(
                                hmx4[:, 0:1], w2natb[parn][:, :], axis=AX.X,
                                op=ALU.max, apply_absolute_value=True), "sv")
                        block.vector(vn1)
                        e_h0 = Y.now("sv")

                        def syn2(sy, ob=ob, parn=parn, e_h0=e_h0):
                            Y.wait(sy, e_h0)
                            e_nld[parn] = (Y.inc(sy.dma_start(
                                out=w2natb[parn][:, :],
                                in_=w2n[ob * 128:(ob + 1) * 128, 4096:8192]),
                                f"k{parn}", 16), None)
                        block.sync(syn2)

                        def vn2(v, ob=ob, parn=parn):
                            Y.wait(v, e_nld[parn][0])
                            v.tensor_reduce(hmx4[:, 1:2], w2natb[parn][:, :],
                                            axis=AX.X, op=ALU.max,
                                            apply_absolute_value=True)
                            vspace(v)
                            v.tensor_reduce(colmax[:, :], hmx4[:, 0:2],
                                            axis=AX.X, op=ALU.max)
                            vspace(v)
                            e_nrd[parn] = Y.inc(v.tensor_scalar(
                                sw2c[:, ob:ob + 1], colmax[:, :], 1.0 / QMAX,
                                Q_EPS, op0=ALU.mult, op1=ALU.max), "sv")
                        block.vector(vn2)
                        e_sw2 = Y.now("sv")

                        def tn(t, ob=ob, e_sw2=e_sw2):
                            Y.wait(t, e_sw2)
                            Y.inc(t.transpose(psZ[0:1, 0:128],
                                              sw2c[:, ob:ob + 1],
                                              identf_sb[:, :]), "st")
                        block.tensor(tn)
                        e_tr2 = Y.now("st")

                        def vn3(v, ob=ob, e_tr2=e_tr2):
                            Y.wait(v, e_tr2)
                            v.tensor_copy(swr2l[0:1, ob * 128:(ob + 1) * 128],
                                          psZ[0:1, 0:128])
                            Y.inc(v.reciprocal(
                                swr2l[0:1, ob * 128:(ob + 1) * 128],
                                swr2l[0:1, ob * 128:(ob + 1) * 128]), "sv")
                        block.vector(vn3)

                    e_wst = [None, None]
                    e_tbl = [None, None]
                    e_tbq = [None, None]
                    for ob in range(16):
                        def tb1(t, ob=ob):
                            Y.wait(t, Y.now("sv"))
                            Y.inc(t.matmul(psC[:, 128:256], ones_rf[:, :],
                                           swr2l[0:1, ob * 128:(ob + 1) * 128],
                                           start=True, stop=True), "st")
                        block.tensor(tb1)
                        e_bc = Y.now("st")

                        def vb1(v, ob=ob, e_bc=e_bc):
                            Y.wait(v, e_bc)
                            Y.inc(v.tensor_copy(rswb2[:, :], psC[:, 128:256]),
                                  "sv")
                        block.vector(vb1)
                        e_rb2 = Y.now("sv")

                        for g in range(8):
                            parw = g % 2

                            def syw(sy, ob=ob, g=g, parw=parw):
                                Y.wait(sy, e_tbq[parw])
                                src3 = w2tb[ob][g * 1024:(g + 1) * 1024, :] \
                                    .rearrange("(c p) f -> p c f", p=128)
                                dst3 = w2tbuf[parw].rearrange(
                                    "p (c f) -> p c f", f=128)
                                e_tbl[parw] = Y.inc(
                                    sy.dma_start(out=dst3, in_=src3),
                                    f"k{2 + parw}", 16)
                            block.sync(syw)

                            def vw(v, ob=ob, g=g, parw=parw, e_rb2=e_rb2):
                                Y.wait(v, e_tbl[parw])
                                Y.wait(v, e_rb2)
                                Y.wait(v, e_wst[parw])
                                last = None
                                for cc in range(8):
                                    last = v._custom_dve(
                                        RND2,
                                        out=wstg[parw][:, cc * 128:
                                                       (cc + 1) * 128],
                                        in0=w2tbuf[parw][:, cc * 128:
                                                         (cc + 1) * 128],
                                        in1=rswb2[:, :], s0=QMAX, imm2=MAGIC)
                                e_tbq[parw] = Y.inc(last, "sv")
                            block.vector(vw)
                            e_q = Y.now("sv")

                            def scw2(s, ob=ob, g=g, parw=parw, e_q=e_q):
                                Y.wait(s, e_q)
                                e_wst[parw] = Y.inc(s.dma_start(
                                    out=qw2s[ob][:, g * 1024:(g + 1) * 1024],
                                    in_=wstg[parw][:, :]), f"k{9 + parw}", 16)
                            block.scalar(scw2)
                barrier()

            def vhalf(v):
                Y.inc(v.tensor_scalar(scal[0:1, 3:4], scal[0:1, 1:2], 0.5,
                                      None, op0=ALU.mult), "sv")
            block.vector(vhalf)
            bcast_cols([(scal[0:1, 3:4], sx1b)])

            def vsc2(v):
                v.tensor_scalar(sc2[:, :], sw2c[:, :], sx1b[:, :], None,
                                op0=ALU.mult)
                Y.inc(v.tensor_scalar(bc2[:, :], b2_sb[:, :], 0.5, None,
                                      op0=ALU.mult), "sv")
            block.vector(vsc2)
            barrier()

            with ExitStack() as ph:
                EE = ph.enter_context
                harena = EE(nc.sbuf_tensor(f"harena_{sub}", [128, 65536], F16))
                wbuf = [EE(nc.sbuf_tensor(f"w2b{i}_{sub}", [128, 4096], F16))
                        for i in range(4)]
                xres = [EE(nc.sbuf_tensor(f"xres{i}_{sub}", [128, 1024], F32))
                        for i in range(2)]
                xout = [EE(nc.sbuf_tensor(f"xout{i}_{sub}", [128, 1024], F32))
                        for i in range(2)]

                e_mm2 = {}
                e_ev2 = {}
                e_xw = {}
                e_hmm = None
                for tp in range(2):
                    def syh(sy, tp=tp):
                        Y.wait(sy, e_hmm)
                        for tl in range(2):
                            tb = tp * 2 + tl
                            A = harena.rearrange("p (i two n) -> p i two n",
                                                 two=2, n=512)
                            Y.inc(sy.dma_start(
                                out=A[:, :, tl, :],
                                in_=hscr[tb].rearrange("(i p) n -> p i n",
                                                       p=128)), "k8", 16)
                    block.sync(syh)
                    e_hld = Y.now("k8")

                    def vrnd(v, tp=tp, e_hld=e_hld):
                        Y.wait(v, e_hld)
                        Y.wait(v, e_hmm)
                        last = None
                        for j in range(64):
                            last = v._custom_dve(
                                RND1, out=harena[:, j * 1024:(j + 1) * 1024],
                                in0=harena[:, j * 1024:(j + 1) * 1024],
                                s0=r2b[:, :], s1=QMAX, imm2=MAGIC)
                        Y.inc(last, "sv")
                    block.vector(vrnd)
                    e_hq = Y.now("sv")

                    for o in range(16):
                        paro = o % 2

                        def syw2(sy, tp=tp, o=o, paro=paro):
                            Y.wait(sy, e_mm2.get((tp, o - 2)))
                            for half in range(2):
                                Y.inc(sy.dma_start(
                                    out=wbuf[2 * paro + half][:, :],
                                    in_=qw2s[o][:, half * 4096:
                                                (half + 1) * 4096]),
                                    f"k{4 + paro}", 16)
                            Y.wait(sy, e_ev2.get((tp, o - 2)))
                            Y.inc(sy.dma_start(
                                out=xres[paro][:, :],
                                in_=src[o * 128:(o + 1) * 128,
                                        tp * 1024:(tp + 1) * 1024]),
                                f"k{4 + paro}", 16)
                        block.sync(syw2)
                        e_wld = Y.now(f"k{4 + paro}")

                        def tm2(t, tp=tp, o=o, paro=paro, e_wld=e_wld,
                                e_hq=e_hq):
                            Y.wait(t, e_wld)
                            Y.wait(t, e_hq)
                            Y.wait(t, e_ev2.get((tp, o - 2)))
                            pb = psB if paro == 0 else psA
                            for ic in range(64):
                                wt = wbuf[2 * paro + ic // 32][
                                    :, (ic % 32) * 128:(ic % 32 + 1) * 128]
                                for tl in range(2):
                                    mm = t.matmul(
                                        pb[:, tl * 512:(tl + 1) * 512], wt,
                                        harena[:, ic * 1024 + tl * 512:
                                               ic * 1024 + (tl + 1) * 512],
                                        start=(ic == 0), stop=(ic == 63))
                            e_mm2[(tp, o)] = Y.inc(mm, "st")
                        block.tensor(tm2)

                        def ve2(v, tp=tp, o=o, paro=paro):
                            Y.wait(v, e_mm2[(tp, o)])
                            Y.wait(v, e_xw.get((tp, o - 2)))
                            pb = psB if paro == 0 else psA
                            last = None
                            for tl in range(2):
                                last = v.affine_then_add(
                                    xout[paro][:, tl * 512:(tl + 1) * 512],
                                    pb[:, tl * 512:(tl + 1) * 512],
                                    xres[paro][:, tl * 512:(tl + 1) * 512],
                                    sc2[:, o:o + 1], bc2[:, o:o + 1])
                            e_ev2[(tp, o)] = Y.inc(last, "sv")
                        block.vector(ve2)

                        def sw2_(s, tp=tp, o=o, paro=paro):
                            Y.wait(s, e_ev2[(tp, o)])
                            e_xw[(tp, o)] = Y.inc(s.dma_start(
                                out=dst[o * 128:(o + 1) * 128,
                                        tp * 1024:(tp + 1) * 1024],
                                in_=xout[paro][:, :]), f"k{6 + paro}", 16)
                        block.scalar(sw2_)
                    e_hmm = e_mm2[(tp, 15)]
            barrier()

        def attn_phase(src, dst, sub=None):
            with ExitStack() as ph0:
                ph = ExitStack()
                EE = ph.enter_context
                y2 = EE(nc.sbuf_tensor("y2", [128, 32768], BF16))
                ln_phase(src, y2, False)
                if sub == "ln2":
                    barrier()
                    ph.close()
                    return

                Kar = EE(nc.sbuf_tensor("Kar", [128, 8192], BF16))
                VT = EE(nc.sbuf_tensor("VT", [128, 8192], BF16))
                Vst = EE(nc.sbuf_tensor("Vst", [128, 2048], BF16))
                Qb = [EE(nc.sbuf_tensor(f"Qb{i}", [128, 2048], BF16))
                      for i in range(2)]
                Ear = [EE(nc.sbuf_tensor(f"Ear{i}", [128, 2048], BF16))
                       for i in range(2)]
                ecl = [EE(nc.sbuf_tensor(f"ecl{i}", [128, 512], F32))
                       for i in range(2)]
                Zp = EE(nc.sbuf_tensor("Zp", [128, 2048], F32))
                Zr = EE(nc.sbuf_tensor("Zr", [1, 2048], F32))
                rbch = EE(nc.sbuf_tensor("rbch", [128, 512], F32))
                ctxst = [EE(nc.sbuf_tensor(f"ctxst{i}", [128, 2048], BF16))
                         for i in range(2)]
                qwb = [EE(nc.sbuf_tensor(f"qwb{i}", [128, 2048], BF16))
                       for i in range(2)]
                Bsum = EE(nc.sbuf_tensor("Bsum", [128, 64], F32))
                SufVs = EE(nc.sbuf_tensor("SufVs", [128, 64], F32))
                ctmp = EE(nc.sbuf_tensor("ctmp", [128, 512], F32))
                dpsb = (EE(nc.sbuf_tensor("dpsb", [128, 512], F32))
                        if (cdbg or h2v is not None) else None)
                dearb = (EE(nc.sbuf_tensor("dearb", [128, 512], BF16))
                         if h2v is not None else None)

                with ExitStack() as kvph:
                    kvwb = [kvph.enter_context(
                        nc.sbuf_tensor(f"kvwb{i}", [128, 2048], BF16))
                        for i in range(2)]

                    e_wld = {}
                    e_gmm = {}
                    e_gev = {}
                    e_vtt = {}
                    e_vte = {}
                    e_suf = {}

                    ngen = 1 if sub == "kv1" else 8
                    for gnum in range(ngen):
                        is_v = gnum >= 4
                        kv = gnum % 4
                        parg = gnum % 2

                        def gld(g, gnum=gnum, is_v=is_v, kv=kv, parg=parg):
                            Y.wait(g, e_gmm.get(gnum - 2))
                            w_hbm = wv if is_v else wk
                            src3 = w_hbm[:, kv * 128:(kv + 1) * 128] \
                                .rearrange("(c p) f -> p c f", p=128)
                            dst3 = kvwb[parg].rearrange("p (c f) -> p c f",
                                                        f=128)
                            e_wld[gnum] = Y.inc(
                                g.dma_start(out=dst3, in_=src3), "g6", 16)
                        block.gpsimd(gld)

                        for tt_ in range(4):
                            def tg(t, gnum=gnum, is_v=is_v, kv=kv,
                                   parg=parg, tt_=tt_):
                                if tt_ == 0:
                                    Y.wait(t, e_wld[gnum])
                                if is_v:
                                    Y.wait(t, e_gev.get((gnum, tt_ - 2)) or
                                           e_gev.get((gnum - 1, tt_ + 2)))
                                    pb, off = psB, (tt_ % 2) * 512
                                else:
                                    Y.wait(t, e_gev.get((gnum - 1, tt_)))
                                    pb, off = psA, tt_ * 512
                                for dc in range(16):
                                    mm = t.matmul(
                                        pb[:, off:off + 512],
                                        kvwb[parg][:, dc * 128:(dc + 1) * 128],
                                        y2[:, dc * 2048 + tt_ * 512:
                                           dc * 2048 + (tt_ + 1) * 512],
                                        start=(dc == 0), stop=(dc == 15))
                                e_gmm[(gnum, tt_)] = Y.inc(mm, "st")
                            block.tensor(tg)

                            def vg(v, gnum=gnum, is_v=is_v, kv=kv, tt_=tt_):
                                if is_v and tt_ == 0:
                                    Y.wait(v, e_vtt.get((kv - 1, 15)))
                                Y.wait(v, e_gmm[(gnum, tt_)])
                                if is_v:
                                    dstap = Vst[:, tt_ * 512:(tt_ + 1) * 512]
                                    srcap = psB[:, (tt_ % 2) * 512:
                                                (tt_ % 2 + 1) * 512]
                                else:
                                    dstap = Kar[:, kv * 2048 + tt_ * 512:
                                                kv * 2048 + (tt_ + 1) * 512]
                                    srcap = psA[:, tt_ * 512:(tt_ + 1) * 512]
                                e_gev[(gnum, tt_)] = Y.inc(
                                    v.tensor_copy(dstap, srcap), "sv")
                            block.vector(vg)
                        e_gmm[gnum] = e_gmm[(gnum, 3)]

                        if is_v and sub != "kv8":
                            for c in range(16):
                                pp = c % 2

                                def trg(pp):
                                    base = ps7 if pp == 0 else psA
                                    return base[:, 0:64].bitcast(BF16)

                                def tv2(t, kv=kv, gnum=gnum, c=c, pp=pp):
                                    if c == 0:
                                        Y.wait(t, e_gev[(gnum, 3)])
                                    Y.wait(t, e_vte.get((kv, c - 2)))
                                    tr = t.transpose(
                                        trg(pp),
                                        Vst[:, c * 128:(c + 1) * 128],
                                        identb_sb[:, :])
                                    e_vtt[(kv, c)] = Y.inc(tr, "st")
                                block.tensor(tv2)

                                def vv2(v, kv=kv, c=c, pp=pp):
                                    Y.wait(v, e_vtt[(kv, c)])
                                    e_vte[(kv, c)] = Y.inc(v.tensor_copy(
                                        VT[:, (kv * 16 + c) * 128:
                                           (kv * 16 + c + 1) * 128],
                                        trg(pp)), "sv")
                                block.vector(vv2)

                            if sub == "kvtr":
                                continue

                            def tv3(t, kv=kv):
                                for c in range(16):
                                    Y.wait(t, e_vte[(kv, c)])
                                    mm = t.matmul(
                                        psC[:, kv * 16 + c:kv * 16 + c + 1],
                                        VT[:, (kv * 16 + c) * 128:
                                           (kv * 16 + c + 1) * 128],
                                        ones_cb[:, :], start=True, stop=True)
                                    e_suf[(kv, c)] = Y.inc(mm, "st")
                            block.tensor(tv3)

                    if sub in ("kv1", "kvt", "kv8", "kvtr"):
                        barrier()
                        kvph.close()
                        ph.close()
                        return

                    def vsuf(v):
                        Y.wait(v, e_suf[(3, 15)])
                        v.tensor_copy(Bsum[:, :], psC[:, 0:64])
                        for kv in range(4):
                            v.tensor_scalar(SufVs[:, kv * 16 + 15:kv * 16 + 16],
                                            Bsum[:, 0:1], 0.0, None,
                                            op0=ALU.mult)
                        for t_ in range(14, -1, -1):
                            for kv in range(4):
                                v.tensor_tensor(
                                    SufVs[:, kv * 16 + t_:kv * 16 + t_ + 1],
                                    SufVs[:, kv * 16 + t_ + 1:
                                          kv * 16 + t_ + 2],
                                    Bsum[:, kv * 16 + t_ + 1:
                                         kv * 16 + t_ + 2], op=ALU.add)
                        vspace(v)
                        vspace(v)
                        Y.inc(v.tensor_scalar(SufVs[:, :], SufVs[:, :], EM,
                                              None, op0=ALU.mult), "sv")
                    block.vector(vsuf)
                barrier()
                if sub == "kv":
                    ph.close()
                    return

                e_wq = {}
                e_qmm = {}
                e_qev = {}
                e_qklast = {}
                e_psB_free = [None, None]
                e_ecl_free = [None, None]
                e_Ear_free = [None, None]
                e_ctx_ev = {}
                e_ctxw = {}
                e_zred_last = {}
                e_rb_last = {}
                e_rdump = {}

                def emit_qprod(h):
                    parq = h % 2

                    def gq(g, h=h, parq=parq):
                        Y.wait(g, e_qmm.get((h - 2, 3)))
                        src3 = wq[:, h * 128:(h + 1) * 128].rearrange(
                            "(c p) f -> p c f", p=128)
                        dst3 = qwb[parq].rearrange("p (c f) -> p c f", f=128)
                        e_wq[h] = Y.inc(g.dma_start(out=dst3, in_=src3),
                                        f"g{2 + parq}", 16)
                    block.gpsimd(gq)

                    for qt in range(4):
                        def tqq(t, h=h, parq=parq, qt=qt):
                            Y.wait(t, e_wq[h])
                            if qt == 0:
                                Y.wait(t, e_qklast.get(h - 2))
                                Y.wait(t, e_qev.get((h - 1, 3)))
                            else:
                                Y.wait(t, e_qev.get((h, qt - 1)))
                            for dc in range(16):
                                mm = t.matmul(
                                    psC[:, 0:512],
                                    qwb[parq][:, dc * 128:(dc + 1) * 128],
                                    y2[:, dc * 2048 + qt * 512:
                                       dc * 2048 + (qt + 1) * 512],
                                    start=(dc == 0), stop=(dc == 15))
                            e_qmm[(h, qt)] = Y.inc(mm, "st")
                        block.tensor(tqq)

                        def vqq(v, h=h, parq=parq, qt=qt):
                            Y.wait(v, e_qmm[(h, qt)])
                            v.tensor_scalar(
                                Qb[parq][:, qt * 512:(qt + 1) * 512],
                                psC[:, 0:512], SQRT_DH, None, op0=ALU.mult)
                            vspace(v)
                            e_qev[(h, qt)] = Y.inc(v.tensor_reduce(
                                vjunk[:, 1:2],
                                Qb[parq][:, qt * 512:(qt + 1) * 512],
                                axis=AX.X, op=ALU.max), "sv")
                        block.vector(vqq)

                emit_qprod(0)
                nheads = (1 if sub in ("h1", "h1d")
                          else 2 if (sub or "").startswith("h2") else NH)
                for h in range(nheads):
                    kv = h // 4
                    parq = h % 2
                    parc = h % 2
                    e_pv_by_c = {}
                    e_zp_last = None

                    skip_cl = (h2v in ("noqk", "noq1", "stt") and h == 1)
                    sk_pv = (h == 1 and h2v in ("nopv", "nozppv", "noexp"))
                    sk_zp = (h == 1 and h2v in ("nozppv", "noexp"))
                    sk_exp = (h == 1 and h2v == "noexp")
                    for c in range(16 if not skip_cl else 0):
                        qs0 = c * 128
                        chunks = []
                        p = qs0
                        while p < S:
                            e_ = min((p // 512 + 1) * 512, S)
                            chunks.append((p, e_))
                            p = e_
                        pare = c % 2
                        e_exp = {}
                        e_qk = {}
                        e_clip = {}
                        e_zpv = {}
                        nch = len(chunks)

                        def mk_tpv(j):
                            p0, p1 = chunks[j]

                            def tpv(t, h=h, c=c, kv=kv, pare=pare, j=j,
                                    p0=p0, p1=p1, e_exp=e_exp, e_zpv=e_zpv):
                                if j == 0 and c == 0:
                                    Y.wait(t, e_ctx_ev.get(h - 1))
                                    if cdbg and h == 1:
                                        Y.wait(t, e_rdump.get("psa"))
                                Y.wait(t, e_zpv[j])
                                b = p0 // 512
                                mm = t.matmul(
                                    psA[:, p0:p1],
                                    VT[:, (kv * 16 + c) * 128:
                                       (kv * 16 + c + 1) * 128],
                                    Ear[pare][:, p0:p1],
                                    start=(c == 0), stop=(c == 4 * b + 3))
                                Y.inc(mm, "st")
                            block.tensor(tpv)

                        for j, (p0, p1) in enumerate(chunks):
                            pb = j % 2

                            def tqk(t, h=h, c=c, kv=kv, parq=parq,
                                    j=j, p0=p0, p1=p1, pb=pb, e_qk=e_qk):
                                if j == 0:
                                    Y.wait(t, e_qev.get((h, 3)))
                                Y.wait(t, e_psB_free[pb])
                                mm = t.matmul(
                                    psB[:, pb * 512:pb * 512 + (p1 - p0)],
                                    Kar[:, kv * 2048 + c * 128:
                                        kv * 2048 + (c + 1) * 128],
                                    Qb[parq][:, p0:p1], start=True, stop=True)
                                e_qk[j] = Y.inc(mm, "st")
                            block.tensor(tqk)

                            def vcl(v, h=h, c=c, j=j, p0=p0, p1=p1, pb=pb,
                                    e_qk=e_qk, e_clip=e_clip):
                                Y.wait(v, e_qk[j])
                                Y.wait(v, e_ecl_free[j % 2])
                                if j == 0:
                                    v.tensor_tensor(
                                        psB[:, pb * 512:pb * 512 + 128],
                                        psB[:, pb * 512:pb * 512 + 128],
                                        maskT_sb[:, :], op=ALU.add)
                                    vspace(v)
                                cl = v.tensor_scalar(
                                    ecl[j % 2][:, 0:(p1 - p0)],
                                    psB[:, pb * 512:pb * 512 + (p1 - p0)],
                                    SOFT_CAP, -SOFT_CAP,
                                    op0=ALU.min, op1=ALU.max)
                                e_clip[j] = Y.inc(cl, "sv")
                                e_psB_free[pb] = e_clip[j]
                            block.vector(vcl)

                            if sk_exp:
                                continue

                            def sxp(s, h=h, c=c, pare=pare, j=j, p0=p0,
                                    p1=p1, e_clip=e_clip, e_exp=e_exp):
                                if j == 0:
                                    Y.wait(s, e_Ear_free[pare])
                                Y.wait(s, e_clip[j])
                                ex = s.activation(
                                    Ear[pare][:, p0:p1],
                                    ecl[j % 2][:, 0:(p1 - p0)], AF.Exp,
                                    bias=shiftb[:, :], scale=1.0)
                                e_exp[j] = Y.inc(ex, "ss")
                                e_ecl_free[j % 2] = e_exp[j]
                            block.scalar(sxp)

                            if sk_zp:
                                continue

                            def vzp(v, h=h, c=c, pare=pare, j=j, p0=p0,
                                    p1=p1, e_exp=e_exp):
                                if j == 0 and c == 0:
                                    Y.wait(v, e_zred_last.get(h - 1))
                                    if cdbg and h == 1:
                                        Y.wait(v, e_rdump.get("zp0"))
                                Y.wait(v, e_exp[j])
                                if c == 0:
                                    last = v.tensor_copy(Zp[:, p0:p1],
                                                         Ear[pare][:, p0:p1])
                                else:
                                    last = v.tensor_tensor(
                                        Zp[:, p0:p1], Zp[:, p0:p1],
                                        Ear[pare][:, p0:p1], op=ALU.add)
                                e_zpv[j] = Y.inc(last, "sv")
                            block.vector(vzp)

                            if j >= 1 and not sk_pv:
                                mk_tpv(j - 1)
                        if h2v is not None and h == 0 and c == 0:
                            def vesnap(v, e_exp=e_exp):
                                Y.wait(v, e_exp[0])
                                Y.inc(v.tensor_copy(dearb[:, :],
                                                    Ear[0][:, 0:512]), "sv")
                            block.vector(vesnap)
                        if not skip_cl and not sk_pv:
                            mk_tpv(nch - 1)
                        e_zp_last = Y.now("sv")
                        e_Ear_free[pare] = Y.now("st")
                    e_qklast[h] = Y.now("st")
                    e_pv_done = Y.now("st")
                    if h2v is not None and h == 0:
                        def vpsd2(v, e_pv_done=e_pv_done):
                            Y.wait(v, e_pv_done)
                            Y.inc(v.tensor_copy(dpsb[:, :], psA[:, 0:512]),
                                  "sv")
                        block.vector(vpsd2)
                        e_psd2 = Y.now("sv")
                    if h2v is not None and h == 1:
                        break

                    if (h + 1 < nheads and h2v not in ("noq1", "stt")) or (
                            h + 1 < NH and h2v is None and sub is None):
                        emit_qprod(h + 1)

                    if cdbg and h == 0:
                        def vpsd(v, e_pv_done=e_pv_done):
                            Y.wait(v, e_pv_done)
                            Y.inc(v.tensor_copy(dpsb[:, :], psA[:, 0:512]),
                                  "sv")
                        block.vector(vpsd)
                        _e_psd = Y.now("sv")

                        def spsd(sy):
                            Y.wait(sy, _e_psd)
                            e_rdump["psa"] = Y.inc(sy.dma_start(
                                out=d_psa, in_=dpsb[:, :]), "kd", 16)
                        block.sync(spsd)

                        def szp0(sy, e_zp_last=e_zp_last):
                            Y.wait(sy, e_zp_last)
                            e_rdump["zp0"] = Y.inc(sy.dma_start(
                                out=d_zp0, in_=Zp[:, :]), "kd", 16)
                        block.sync(szp0)
                    e_zred = {}
                    e_zev = {}
                    for j in range(8):
                        def tz(t, h=h, j=j, e_zp_last=e_zp_last,
                               e_zred=e_zred, e_zev=e_zev):
                            if j == 0:
                                Y.wait(t, e_zp_last)
                            Y.wait(t, e_zev.get(j - 1))
                            mm = t.matmul(psZ[0:1, 0:256], ones_cf[:, :],
                                          Zp[:, j * 256:(j + 1) * 256],
                                          start=True, stop=True)
                            e_zred[j] = Y.inc(mm, "st")
                        block.tensor(tz)

                        def vz(v, h=h, j=j, e_zred=e_zred, e_zev=e_zev):
                            if j == 0:
                                Y.wait(v, e_rb_last.get(h - 1))
                                if cdbg:
                                    Y.wait(v, e_rdump.get(h - 1))
                            Y.wait(v, e_zred[j])
                            e_zev[j] = Y.inc(v.tensor_copy(
                                Zr[0:1, j * 256:(j + 1) * 256],
                                psZ[0:1, 0:256]), "sv")
                        block.vector(vz)
                    e_zred_last[h] = e_zred[7]

                    def vzf(v, h=h):
                        vspace(v)
                        v.tensor_tensor(Zr[:, :], Zr[:, :], cnt_sb[:, :],
                                        op=ALU.add)
                        vspace(v)
                        v.reciprocal(Zr[:, :], Zr[:, :])
                        vspace(v)
                        Y.inc(v.tensor_reduce(vjunk[0:1, 0:1], Zr[:, :],
                                              axis=AX.X, op=ALU.max), "sv")
                    block.vector(vzf)
                    e_rrow = Y.now("sv")
                    if cdbg:
                        def srd(sy, h=h, e_rrow=e_rrow):
                            Y.wait(sy, e_rrow)
                            e_rdump[h] = Y.inc(sy.dma_start(
                                out=d_rall[h], in_=Zr[:, :]), "kd", 16)
                        block.sync(srd)

                    e_rbm = {}
                    e_rbe = {}
                    e_sxc = {}
                    e_vct = {}
                    for j in range(4):
                        pb = j % 2

                        def trb(t, h=h, j=j, pb=pb, e_rrow=e_rrow,
                                e_rbm=e_rbm, e_rbe=e_rbe):
                            if j == 0:
                                Y.wait(t, e_rrow)
                            Y.wait(t, e_psB_free[pb])
                            Y.wait(t, e_rbe.get(j - 2))
                            mm = t.matmul(psB[:, pb * 512:(pb + 1) * 512],
                                          ones_rf[:, :],
                                          Zr[0:1, j * 512:(j + 1) * 512],
                                          start=True, stop=True)
                            e_rbm[j] = Y.inc(mm, "st")
                        block.tensor(trb)

                        def sxc(v, h=h, kv=kv, j=j,
                                e_pv_done=e_pv_done, e_sxc=e_sxc,
                                e_vct=e_vct):
                            if h2v == "stt":
                                if j == 0:
                                    Y.wait(v, e_pv_done)
                                return
                            if j == 0:
                                Y.wait(v, e_pv_done)
                            Y.wait(v, e_vct.get(j - 1))
                            last = None
                            for tq_ in range(4 * j, 4 * j + 4):
                                last = v.tensor_scalar(
                                    ctmp[:, (tq_ % 4) * 128:
                                         (tq_ % 4 + 1) * 128],
                                    psA[:, tq_ * 128:(tq_ + 1) * 128],
                                    SufVs[:, kv * 16 + tq_:
                                          kv * 16 + tq_ + 1],
                                    None, op0=ALU.add)
                            e_sxc[j] = Y.inc(last, "sv")
                        block.vector(sxc)

                        if cdbg and h == 0 and j == 0:
                            def vrbd(v, e_rbm=e_rbm):
                                Y.wait(v, e_rbm[0])
                                Y.wait(v, e_rdump.get("psa"))
                                Y.inc(v.tensor_copy(dpsb[:, :],
                                                    psB[:, 0:512]), "sv")
                            block.vector(vrbd)
                            _e_rbd = Y.now("sv")

                            def srbd(sy):
                                Y.wait(sy, _e_rbd)
                                e_rdump["rb"] = Y.inc(sy.dma_start(
                                    out=d_rb, in_=dpsb[:, :]), "kd", 16)
                            block.sync(srbd)

                        def vev(v, h=h, kv=kv, parc=parc, j=j, pb=pb,
                                e_rbm=e_rbm, e_rbe=e_rbe, e_sxc=e_sxc,
                                e_vct=e_vct):
                            if j == 0:
                                Y.wait(v, e_ctxw.get(h - 2))
                                if cdbg and h == 0:
                                    Y.wait(v, e_rdump.get("rb"))
                            Y.wait(v, e_rbm[j])
                            rb = v.tensor_copy(rbch[:, :],
                                               psB[:, pb * 512:(pb + 1) * 512])
                            e_rbe[j] = Y.inc(rb, "sv")
                            e_psB_free[pb] = e_rbe[j]
                            vspace(v)
                            last = None
                            if h2v == "stt":
                                for tq_ in range(4 * j, 4 * j + 4):
                                    last = v.scalar_tensor_tensor(
                                        ctxst[parc][:, tq_ * 128:
                                                    (tq_ + 1) * 128],
                                        psA[:, tq_ * 128:(tq_ + 1) * 128],
                                        SufVs[:, kv * 16 + tq_:
                                              kv * 16 + tq_ + 1],
                                        rbch[:, (tq_ % 4) * 128:
                                             (tq_ % 4 + 1) * 128],
                                        op0=ALU.add, op1=ALU.mult)
                            else:
                                Y.wait(v, e_sxc[j])
                                for tq_ in range(4 * j, 4 * j + 4):
                                    last = v.tensor_tensor(
                                        ctxst[parc][:, tq_ * 128:
                                                    (tq_ + 1) * 128],
                                        ctmp[:, (tq_ % 4) * 128:
                                             (tq_ % 4 + 1) * 128],
                                        rbch[:, (tq_ % 4) * 128:
                                             (tq_ % 4 + 1) * 128],
                                        op=ALU.mult)
                            e_vct[j] = Y.inc(last, "sv")
                            if j == 3:
                                e_ctx_ev[h] = e_vct[j]
                        block.vector(vev)
                    e_rb_last[h] = e_rbm[3]

                    def scx(s, h=h, parc=parc):
                        Y.wait(s, e_ctx_ev[h])
                        e_ctxw[h] = Y.inc(s.dma_start(
                            out=ctxs[h], in_=ctxst[parc][:, :]),
                            f"k{6 + parc}", 16)
                    block.scalar(scx)
                barrier()
                if h2v is not None:
                    def sdh2(sy):
                        Y.inc(sy.dma_start(out=d_cls2, in_=ctxst[0][:, :]),
                              "qs", 16)
                        Y.inc(sy.dma_start(out=d_zr2, in_=Zr[:, :]),
                              "qs", 16)
                        Y.inc(sy.dma_start(out=d_psa2, in_=dpsb[:, :]),
                              "qs", 16)
                        Y.inc(sy.dma_start(out=d_earb, in_=dearb[:, :]),
                              "qs", 16)
                        Y.wait(sy, Y.now("qs"))
                    block.sync(sdh2)
                    ph.close()
                    return
                if cdbg:
                    def sdcl(sy):
                        Y.inc(sy.dma_start(out=d_cls, in_=ctxst[1][:, :]),
                              "kd", 16)
                        Y.inc(sy.dma_start(out=d_suf2, in_=SufVs[:, :]),
                              "kd", 16)
                        Y.wait(sy, Y.now("kd"))
                    block.sync(sdcl)
                if sub in ("h1", "h1d", "heads"):
                    if sub == "h1d":
                        def ddump(sy):
                            for sn, cv in Y.c.items():
                                if cv:
                                    sy.wait_ge(Y.sems[sn], cv)
                            Y.inc(sy.dma_start(out=d_qb, in_=Qb[0][:, :]),
                                  "qs", 16)
                            Y.inc(sy.dma_start(out=d_kar,
                                               in_=Kar[:, 0:2048]), "qs", 16)
                            Y.inc(sy.dma_start(out=d_vt,
                                               in_=VT[:, 0:2048]), "qs", 16)
                            Y.inc(sy.dma_start(out=d_zp, in_=Zp[:, :]),
                                  "qs", 16)
                            Y.inc(sy.dma_start(out=d_zr, in_=Zr[:, :]),
                                  "qs", 16)
                            Y.inc(sy.dma_start(out=d_ctx,
                                               in_=ctxst[0][:, :]), "qs", 16)
                            Y.inc(sy.dma_start(out=d_suf,
                                               in_=SufVs[:, 0:16]), "qs", 16)
                        block.sync(ddump)
                    ph.close()
                    return
                ph.close()

                with ExitStack() as woph:
                    E2 = woph.enter_context
                    wob = [E2(nc.sbuf_tensor(f"wob{i}", [128, 2048], BF16))
                           for i in range(4)]
                    ctxl = [E2(nc.sbuf_tensor(f"ctxl{i}", [128, 1024], BF16))
                            for i in range(4)]
                    xr2 = [E2(nc.sbuf_tensor(f"xr2{i}", [128, 2048], F32))
                           for i in range(4)]
                    xo2 = [E2(nc.sbuf_tensor(f"xo2{i}", [128, 2048], F32))
                           for i in range(4)]

                    e_womm = {}
                    e_woev = {}
                    e_vwo = {}
                    e_wow = {}
                    e_wol = {}
                    e_cl = {}
                    for op_ in range(8):
                        parw = op_ % 2

                        def gwo(g, op_=op_, parw=parw):
                            Y.wait(g, e_womm.get((op_ - 2, 1, 15)))
                            for oo in range(2):
                                o = op_ * 2 + oo
                                src3 = wo[:, o * 128:(o + 1) * 128].rearrange(
                                    "(c p) f -> p c f", p=128)
                                dst3 = wob[2 * parw + oo].rearrange(
                                    "p (c f) -> p c f", f=128)
                                Y.inc(g.dma_start(out=dst3, in_=src3),
                                      f"g{4 + parw}", 16)
                            e_wol[op_] = Y.now(f"g{4 + parw}")
                        block.gpsimd(gwo)

                        def sxr(sy, op_=op_, parw=parw):
                            for oo in range(2):
                                o = op_ * 2 + oo
                                Y.wait(sy, e_woev.get((op_ - 2, oo)))
                                Y.inc(sy.dma_start(
                                    out=xr2[2 * parw + oo][:, :],
                                    in_=src[o * 128:(o + 1) * 128, :]),
                                    f"k{9 + parw}", 16)
                            e_wol[(op_, "x")] = Y.now(f"k{9 + parw}")
                        block.sync(sxr)

                        for th in range(2):
                            for dc in range(16):
                                def sctx(sy, op_=op_, th=th, dc=dc):
                                    Y.wait(sy, e_womm.get((op_, th, dc - 2))
                                           if dc >= 2 else
                                           e_womm.get((op_ - 1, th, 14 + dc)))
                                    ksl = ("k0", "k1", "k8", "k11")[
                                        2 * th + dc % 2]
                                    Y.inc(sy.dma_start(
                                        out=ctxl[2 * th + dc % 2][:, :],
                                        in_=ctxs[dc][:, th * 1024:
                                                     (th + 1) * 1024]), ksl, 16)
                                    e_cl[(op_, th, dc)] = Y.now(ksl)
                                block.sync(sctx)

                                def tmo(t, op_=op_, th=th, dc=dc, parw=parw):
                                    if dc == 0:
                                        Y.wait(t, e_wol[op_])
                                        Y.wait(t, e_vwo.get((op_, th - 1))
                                               if th else
                                               e_vwo.get((op_ - 1, 1)))
                                    Y.wait(t, e_cl[(op_, th, dc)])
                                    for oo in range(2):
                                        for tl in range(2):
                                            mm = t.matmul(
                                                psA[:, (oo * 2 + tl) * 512:
                                                    (oo * 2 + tl + 1) * 512],
                                                wob[2 * parw + oo][
                                                    :, dc * 128:(dc + 1) * 128],
                                                ctxl[2 * th + dc % 2][
                                                    :, tl * 512:(tl + 1) * 512],
                                                start=(dc == 0),
                                                stop=(dc == 15))
                                    e_womm[(op_, th, dc)] = Y.inc(mm, "st")
                                block.tensor(tmo)

                            def vwo(v, op_=op_, th=th, parw=parw):
                                Y.wait(v, e_womm[(op_, th, 15)])
                                Y.wait(v, e_wol[(op_, "x")])
                                last = None
                                for oo in range(2):
                                    Y.wait(v, e_wow.get((op_ - 2, oo)))
                                    for tl in range(2):
                                        last = v.tensor_tensor(
                                            xo2[2 * parw + oo][
                                                :, th * 1024 + tl * 512:
                                                th * 1024 + (tl + 1) * 512],
                                            psA[:, (oo * 2 + tl) * 512:
                                                (oo * 2 + tl + 1) * 512],
                                            xr2[2 * parw + oo][
                                                :, th * 1024 + tl * 512:
                                                th * 1024 + (tl + 1) * 512],
                                            op=ALU.add)
                                    if th == 1:
                                        e_woev[(op_, oo)] = Y.inc(last, "sv")
                                e_vwo[(op_, th)] = (e_woev[(op_, 1)]
                                                    if th == 1
                                                    else Y.inc(last, "sv"))
                            block.vector(vwo)

                        def swo(s, op_=op_, parw=parw):
                            for oo in range(2):
                                o = op_ * 2 + oo
                                Y.wait(s, e_woev[(op_, oo)])
                                e_wow[(op_, oo)] = Y.inc(s.dma_start(
                                    out=dst[o * 128:(o + 1) * 128, :],
                                    in_=xo2[2 * parw + oo][:, :]),
                                    f"k{6 + parw}", 16)
                        block.scalar(swo)
            barrier()

        if phases.startswith("attnb"):
            sub = phases.split("_", 1)[1] if "_" in phases else None
            attn_phase(xT, yout, sub=None if sub == "full" else sub)
            def fin0(sy):
                for sn, cv in Y.c.items():
                    if cv:
                        sy.wait_ge(Y.sems[sn], cv)
            block.sync(fin0)
            top.close()
            return nc

        sl1 = ExitStack()
        yar = sl1.enter_context(nc.sbuf_tensor("yar1", [128, 32768], F16))
        ln_phase(xT, yar, True)
        finalize_max(ymaxc, 64)
        allreduce_scale()
        bcast_cols([(scal[0:1, 2:3], r1b), (scal[0:1, 1:2], sx1b)])
        barrier()
        dst1 = yout if phases.startswith("ffn1_only") else xscr
        ffn_phase(1, yar, xT, dst1, y_stack=sl1)

        if not phases.startswith("ffn1_only"):
            dst2 = yout if phases.startswith("upto_attn") else xscr
            attn_phase(xscr, dst2)
            if cdbg:
                def cdump(sy):
                    for sn, cv in Y.c.items():
                        if cv:
                            sy.wait_ge(Y.sems[sn], cv)
                    for hh in range(16):
                        Y.inc(sy.dma_start(out=d_call[hh], in_=ctxs[hh]),
                              "qs", 16)
                block.sync(cdump)
            if not phases.startswith("upto_attn"):
                sl3 = ExitStack()
                yar3 = sl3.enter_context(
                    nc.sbuf_tensor("yar3", [128, 32768], F16))
                ln_phase(xscr, yar3, True)
                finalize_max(ymaxc, 64)
                allreduce_scale()
                bcast_cols([(scal[0:1, 2:3], r1b),
                            (scal[0:1, 1:2], sx1b)])
                barrier()
                ffn_phase(3, yar3, xscr, yout, y_stack=sl3)

        if dbg:
            def dbgout(sy):
                for sn in ("qc", "qs", "qp", "st", "sv", "ss"):
                    if Y.c[sn]:
                        sy.wait_ge(Y.sems[sn], Y.c[sn])
                Y.inc(sy.dma_start(out=dbg_scal[:, :], in_=scal[0:1, 0:8]),
                      "qs", 16)
                Y.inc(sy.dma_start(out=dbg_sw1[:, :], in_=sw1c[:, :]),
                      "qs", 16)
                Y.inc(sy.dma_start(out=dbg_h[:, :], in_=hscr[0]), "qs", 16)
                with nc.allow_non_contiguous_dma(reason="debug"):
                    Y.inc(sy.dma_start(out=dbg_r[:, 0:1], in_=r1b[:, :]), "qs", 16)
                    Y.inc(sy.dma_start(out=dbg_r[:, 1:2], in_=sx1b[:, :]), "qs", 16)
                    Y.inc(sy.dma_start(out=dbg_r[:, 2:3], in_=r2b[:, :]), "qs", 16)
                    Y.inc(sy.dma_start(out=dbg_r[:, 3:4], in_=sgbuf[:, 0:1]), "qs", 16)
            block.sync(dbgout)

        def fin(sy):
            for sn, cv in Y.c.items():
                if cv:
                    sy.wait_ge(Y.sems[sn], cv)
        block.sync(fin)

    return nc


_CACHE = {}


def _get_nc(phases):
    if phases not in _CACHE:
        nc = build_nc(phases)
        lower_extended_insts(nc)
        _CACHE[phases] = nc
    return _CACHE[phases]


def make_in_maps(x, Wq, Wk, Wv, Wo, w1, b1, w2, b2):
    f32 = np.float32
    wq_ = np.ascontiguousarray(np.asarray(Wq, f32))
    wk_ = np.ascontiguousarray(np.asarray(Wk, f32))
    wv_ = np.ascontiguousarray(np.asarray(Wv, f32))
    wo_ = np.ascontiguousarray(np.asarray(Wo, f32))
    w1n = np.ascontiguousarray(np.asarray(w1, f32))
    w2n = np.ascontiguousarray(np.asarray(w2, f32))
    w1tb = np.ascontiguousarray(w1n.T.reshape(D, 64, 128).transpose(1, 0, 2))
    w2tb = np.ascontiguousarray(w2n.T.reshape(I, 16, 128).transpose(1, 0, 2))
    b1r = np.ascontiguousarray(np.asarray(b1, f32).reshape(64, 128).T)
    b2r = np.ascontiguousarray(np.asarray(b2, f32).reshape(16, 128).T)
    maskT = np.zeros((128, 128), f32)
    kk, qq = np.meshgrid(np.arange(128), np.arange(128), indexing="ij")
    maskT[kk > qq] = -1e9
    q_idx = np.arange(S)
    cnt = ((S - (q_idx // 128 + 1) * 128).astype(np.float64)
           * np.exp(np.float64(-SOFT_CAP + EXP_SHIFT))).astype(f32)[None, :]
    identb_ = np.eye(128, dtype=ml_dtypes.bfloat16)
    identf_ = np.eye(128, dtype=f32)

    maps = []
    for c in range(NCORES):
        xTc = np.ascontiguousarray(np.asarray(x[c], f32).T)
        maps.append(dict(xT=xTc, wq=wq_, wk=wk_, wv=wv_, wo=wo_, w1n=w1n,
                         w1tb=w1tb, w2n=w2n, w2tb=w2tb, b1r=b1r, b2r=b2r,
                         maskT=maskT, cntrow=cnt, identb=identb_,
                         identf=identf_))
    return maps


def run(inputs, phases="full"):
    maps = make_in_maps(
        inputs["L0_attn_hidden_states"], inputs["L0_attn_Wq"],
        inputs["L0_attn_Wk"], inputs["L0_attn_Wv"], inputs["L0_attn_Wo"],
        inputs["L0_ffn_w1"], inputs["L0_ffn_b1"], inputs["L0_ffn_w2"],
        inputs["L0_ffn_b2"])
    nc = _get_nc(phases)
    res = run_bass_kernel_spmd(nc, maps, core_ids=list(range(NCORES)))
    out = np.stack([np.ascontiguousarray(res.results[c]["yout"]).T
                    for c in range(NCORES)], axis=0)
    return np.ascontiguousarray(out, dtype=np.float32)


def kernel(L0_attn_hidden_states, L0_attn_Wq, L0_attn_Wk, L0_attn_Wv,
           L0_attn_Wo, L0_ffn_w1, L0_ffn_b1, L0_ffn_w2, L0_ffn_b2):
    return run(dict(
        L0_attn_hidden_states=L0_attn_hidden_states, L0_attn_Wq=L0_attn_Wq,
        L0_attn_Wk=L0_attn_Wk, L0_attn_Wv=L0_attn_Wv, L0_attn_Wo=L0_attn_Wo,
        L0_ffn_w1=L0_ffn_w1, L0_ffn_b1=L0_ffn_b1, L0_ffn_w2=L0_ffn_w2,
        L0_ffn_b2=L0_ffn_b2), phases="full")
